# version 03
# baseline (speedup 1.0000x reference)
import numpy as np
import ml_dtypes
from contextlib import ExitStack

import concourse.bass as bass
import concourse.mybir as mybir
from concourse.bass_utils import run_bass_kernel_spmd

F32 = mybir.dt.float32
BF16 = mybir.dt.bfloat16
I32 = mybir.dt.int32
AF = mybir.ActivationFunctionType
ALU = mybir.AluOpType

S = 4096; DM = 2048; NCD = 16; H = 16; DH = 128; DL = 512
HID = 5632; NHC = 44; DIN = 4096; NPROJ = 8384
NH = 64; HD = 64; CH = 128; NCH = 32
TT = 512; NTT = 8
SM_SHIFT = 30.0
ISQ = float(1.0 / np.sqrt(128.0))
LN_EPS = 1e-5
BUILD_UPTO = [5]
DBG_NCH = [NCH]
DBG_STAGE = [99]
DBG_DUMP = [False]


class _Rec:
    __slots__ = ("attr", "args", "kwargs", "incs")

    def __init__(self, attr, args, kwargs):
        self.attr, self.args, self.kwargs, self.incs = attr, args, kwargs, []

    def then_inc(self, sem, val):
        self.incs.append((sem, val))
        return self


class EngQ:
    def __init__(self, name):
        self.name = name
        self.q = []

    def __getattr__(self, attr):
        if attr.startswith("_"):
            raise AttributeError(attr)

        def call(*args, **kwargs):
            r = _Rec(attr, args, kwargs)
            self.q.append(r)
            return r
        return call

    def emit(self, eng):
        for i, r in enumerate(self.q):
            try:
                ins = getattr(eng, r.attr)(*r.args, **r.kwargs)
            except Exception as e:
                raise RuntimeError(
                    f"{self.name} inst {i}: {r.attr} args="
                    f"{[getattr(a, 'shape', a) for a in r.args]}") from e
            for sem, val in r.incs:
                ins.then_inc(sem, val)


class Shared:
    def __init__(self, handle):
        self.h = handle
        self.n = 0


class Role:
    def __init__(self, shared, step=1):
        self.s = shared
        self.step = step
        self.marks = []

    @property
    def h(self):
        return self.s.h

    @property
    def n(self):
        return self.s.n

    def inc(self, rec):
        rec.then_inc(self.s.h, self.step)
        self.s.n += self.step
        self.marks.append(self.s.n)
        return self.s.n

    def val(self, i):
        if i < 0:
            i = len(self.marks) + i
        if 0 <= i < len(self.marks):
            return self.marks[i]
        return 0

    def count(self):
        return len(self.marks)


def build_program():
    upto = BUILD_UPTO[0]
    nc = bass.Bass("TRN2", target_bir_lowering=False)
    gctx = ExitStack()

    def dram(name, shape, dtype, kind="Internal"):
        return nc.dram_tensor(name, list(shape), dtype, kind=kind).ap()

    x_in = dram("x_fm", (DM, S), F32, "ExternalInput")
    wqT = dram("wqT", (DM, DM), F32, "ExternalInput")
    wdT = dram("wdT", (DM, DL), F32, "ExternalInput")
    wukT = dram("wukT", (H, DL, DH), F32, "ExternalInput")
    wuvT = dram("wuvT", (H, DL, DH), F32, "ExternalInput")
    woT = dram("woT", (DM, DM), F32, "ExternalInput")
    winT = dram("winT", (DM, NPROJ), F32, "ExternalInput")
    woutT = dram("woutT", (DIN, DM), F32, "ExternalInput")
    g_int = dram("g_int", (DM, HID), I32, "ExternalInput")
    u_int = dram("u_int", (DM, HID), I32, "ExternalInput")
    d_int = dram("d_int", (HID, DM), I32, "ExternalInput")
    sg_c = dram("sg_c", (128, NHC), F32, "ExternalInput")
    su_c = dram("su_c", (128, NHC), F32, "ExternalInput")
    sd_c = dram("sd_c", (128, NCD), F32, "ExternalInput")
    convw_c = dram("convw_c", (128, 33, 4), F32, "ExternalInput")
    convb_c = dram("convb_c", (128, 33), F32, "ExternalInput")
    dtb_col = dram("dtb_col", (NH, 1), F32, "ExternalInput")
    alog_col = dram("alog_col", (NH, 1), F32, "ExternalInput")
    dD_big = dram("dD_big", (128, NCH), F32, "ExternalInput")
    normw_c = dram("normw_c", (128, 32), F32, "ExternalInput")
    c_ones_col = dram("c_ones_col", (128, 1), BF16, "ExternalInput")
    c_ones_row_f = dram("c_ones_row_f", (1, 128), F32, "ExternalInput")
    c_ones64_128 = dram("c_ones64_128", (64, 128), BF16, "ExternalInput")
    c_ones64_64 = dram("c_ones64_64", (64, 64), BF16, "ExternalInput")
    c_ident = dram("c_ident", (128, 128), BF16, "ExternalInput")
    c_ut = dram("c_ut", (128, 128), BF16, "ExternalInput")
    c_ltbig = dram("c_ltbig", (128, 128), BF16, "ExternalInput")
    c_i4 = dram("c_i4", (128, 512), BF16, "ExternalInput")
    c_e128 = dram("c_e128", (64, 8192), BF16, "ExternalInput")
    c_e64 = dram("c_e64", (64, 4096), BF16, "ExternalInput")
    c_parity = dram("c_parity", (64, 128), BF16, "ExternalInput")
    c_pairmask = dram("c_pairmask", (64, 4096), BF16, "ExternalInput")
    c_fconst = dram("c_fconst", (128, 4), F32, "ExternalInput")

    out_fm = dram("out_fm", (DM, S), F32, "ExternalOutput")
    if DBG_DUMP[0]:
        zx_out = dram("zx_out", (8320, S), BF16, "ExternalOutput")
        dtr_out = dram("dtr_out", (NH, S), F32, "ExternalOutput")
        yn_out = dram("yn_out", (DIN, S), BF16, "ExternalOutput")
        dbg_xc = dram("dbg_xc", (128, 33, 128), BF16, "ExternalOutput")
        dbg_sc = dram("dbg_sc", (64, 8, 128), F32, "ExternalOutput")
        dbg_mt = dram("dbg_mt", (128, 32, 128), BF16, "ExternalOutput")
        dbg_xtm = dram("dbg_xtm", (128, 32, 128), BF16, "ExternalOutput")
        dbg_my = dram("dbg_my", (128, 32, 128), BF16, "ExternalOutput")
        dbg_ptp = dram("dbg_ptp", (128, 32, 128), BF16, "ExternalOutput")
        dbg_g0 = dram("dbg_g0", (128, 128), BF16, "ExternalOutput")
        dbg_me2 = dram("dbg_me2", (128, 512), F32, "ExternalOutput")
        dbg_sb = dram("dbg_sb", (64, 4096), BF16, "ExternalOutput")

    q_scr = dram("q_scr", (DM, S), BF16)
    o_scr = dram("o_scr", (DM, S), BF16)
    x1_scr = dram("x1_scr", (DM, S), F32)
    x2_scr = dram("x2_scr", (DM, S), F32)
    x3_scr = dram("x3_scr", (DM, S), F32)
    xln_scr = dram("xln_scr", (DM, S), BF16)
    gu_scr = dram("gu_scr", (HID, S), BF16)
    zx_scr = dram("zx_scr", (8320, S), BF16)
    dtr_scr = dram("dtr_scr", (NH, S), F32)
    yn_scr = dram("yn_scr", (DIN, S), BF16)

    SP = EngQ("sync"); PO = EngQ("gpsimd"); TE = EngQ("tensor")
    DV = EngQ("vector"); SC = EngQ("scalar")
    ALLQ = [SP, PO, TE, DV, SC]

    banks = [gctx.enter_context(nc.psum_tensor(f"bank{i}", [128, 512], F32))
             for i in range(8)]

    shr = {q.name: Shared(gctx.enter_context(nc.semaphore("ms_" + q.name)))
           for q in ALLQ}
    _roles = {}

    def role(name, engq):
        if name not in _roles:
            _roles[name] = Role(shr[engq.name], 1)
        return _roles[name]

    def dsem(name):
        if name not in _roles:
            _roles[name] = Role(Shared(gctx.enter_context(nc.semaphore(name))), 16)
        return _roles[name]

    phase = Shared(gctx.enter_context(nc.semaphore("phase")))

    def barrier():
        for q in ALLQ:
            q.drain()
            q.sem_inc(phase.h, 1)
        phase.n += 5
        for q in ALLQ:
            q.wait_ge(phase.h, phase.n)

    def sbuf(ctx, name, shape, dtype):
        return ctx.enter_context(nc.sbuf_tensor(name, list(shape), dtype))

    k_ones_col = sbuf(gctx, "k_ones_col", (128, 1), BF16)
    k_ones_row_f = sbuf(gctx, "k_ones_row_f", (1, 128), F32)
    k_ident = sbuf(gctx, "k_ident", (128, 128), BF16)
    kfc = sbuf(gctx, "kfc", (128, 4), F32)
    sLdC = dsem("sLdC")
    for dst, src in ((k_ones_col, c_ones_col), (k_ones_row_f, c_ones_row_f),
                     (k_ident, c_ident), (kfc, c_fconst)):
        SP.dma_start(out=dst[:], in_=src)
        sLdC.inc(SP.q[-1])
    for q in ALLQ:
        q.wait_ge(sLdC.h, sLdC.n)

    aSq = role("aSq", SC); tStat = role("tStat", TE)
    dRow = role("dRow", DV); aRow = role("aRow", SC)
    tBc = role("tBc", TE); dXln = role("dXln", DV)

    def emit_ln(sxb_ap, sxq_t, sxln_ap, rows_t, nch=NCD):
        bA, bB, bC, bD = banks[0], banks[1], banks[2], banks[3]
        half = nch // 2
        SC.activation(sxq_t[:, 0:half, :], sxb_ap[:, 0:half, :], AF.Square)
        aSq.inc(SC.q[-1])
        SC.activation(sxq_t[:, half:nch, :], sxb_ap[:, half:nch, :], AF.Square)
        aSq.inc(SC.q[-1])
        for c in range(nch):
            TE.matmul(bA[0:1, :], k_ones_col[:], sxb_ap[:, c, :],
                      start=(c == 0), stop=(c == nch - 1))
        TE.wait_ge(aSq.h, aSq.val(-2))
        for c in range(half):
            TE.matmul(bB[0:1, :], k_ones_col[:], sxq_t[:, c, :],
                      start=(c == 0), stop=False)
        TE.wait_ge(aSq.h, aSq.val(-1))
        for c in range(half, nch):
            TE.matmul(bB[0:1, :], k_ones_col[:], sxq_t[:, c, :],
                      start=False, stop=(c == nch - 1))
        tStat.inc(TE.q[-1])
        r_m, r_var, r_rstd = rows_t
        DV.wait_ge(tStat.h, tStat.n)
        inv = 1.0 / (nch * 128)
        DV.tensor_scalar_mul(r_m, bA[0:1, :], inv)
        DV.tensor_tensor(r_var, r_m, r_m, ALU.mult)
        DV.scalar_tensor_tensor(r_var, bB[0:1, :], inv, r_var,
                                ALU.mult, ALU.subtract)
        dRow.inc(DV.q[-1])
        SC.wait_ge(dRow.h, dRow.n)
        SC.activation(r_rstd, r_var, AF.Ln, bias=kfc[0:1, 0:1])
        SC.activation(r_rstd, r_rstd, AF.Exp, scale=-0.5)
        aRow.inc(SC.q[-1])
        TE.wait_ge(aRow.h, aRow.n)
        TE.matmul(bC[:, :], k_ones_row_f[:], r_m, start=True, stop=True)
        TE.matmul(bD[:, :], k_ones_row_f[:], r_rstd, start=True, stop=True)
        tBc.inc(TE.q[-1])
        DV.wait_ge(tBc.h, tBc.n)
        for c in range(nch):
            DV.tensor_tensor(sxln_ap[:, c, :], sxb_ap[:, c, :], bC[:, :], ALU.subtract)
            DV.tensor_tensor(sxln_ap[:, c, :], sxln_ap[:, c, :], bD[:, :], ALU.mult)
        dXln.inc(DV.q[-1])

    p01 = ExitStack()
    sckv = sbuf(p01, "sckv", (128, 4, S), BF16)
    p0 = ExitStack()
    sxb = sbuf(p0, "sxb", (128, 2, NCD, TT), BF16)
    sxq = sbuf(p0, "sxq", (128, NCD, TT), BF16)
    sxln = sbuf(p0, "sxln", (128, NCD, TT), BF16)
    sqstage = sbuf(p0, "sqstage", (128, H, TT), BF16)
    swq = sbuf(p0, "swq", (128, NCD, DM), BF16)
    swd = sbuf(p0, "swd", (128, NCD, DL), BF16)
    srows = sbuf(p0, "srows", (1, 3, TT), F32)

    sLdB = dsem("sLdB"); sW0 = dsem("sW0"); sQst = dsem("sQst")
    tQg = role("tQg", TE); aQev = role("aQev", SC)

    PO.dma_start(out=swq[:], in_=wqT.rearrange("(c p) j -> p c j", p=128))
    sW0.inc(PO.q[-1])
    PO.dma_start(out=swd[:], in_=wdT.rearrange("(c p) l -> p c l", p=128))
    sW0.inc(PO.q[-1])

    for tt in range(NTT):
        sl = tt % 2
        ts = slice(tt * TT, (tt + 1) * TT)
        if tt >= 2:
            PO.wait_ge(dXln.h, dXln.val(tt - 2))
        PO.dma_start(out=sxb[:, sl],
                     in_=x_in.rearrange("(c p) t -> p c t", p=128)[:, :, ts])
        sLdB.inc(PO.q[-1])
        SC.wait_ge(sLdB.h, sLdB.n)
        TE.wait_ge(sLdB.h, sLdB.n)
        if tt > 0:
            TE.wait_ge(dRow.h, dRow.n)
            TE.wait_ge(dXln.h, dXln.n)
            SC.wait_ge(tStat.h, tStat.n)
            DV.wait_ge(tQg.h, tQg.n)
        emit_ln(sxb[:, sl], sxq, sxln,
                (srows[:, 0, :], srows[:, 1, :], srows[:, 2, :]))
        TE.wait_ge(dXln.h, dXln.n)
        if tt == 0:
            TE.wait_ge(sW0.h, sW0.n)
        SC.wait_ge(sQst.h, sQst.val(tt - 1))
        qe0 = aQev.count()
        for g in range(H + 4):
            bank = banks[4 + g % 4]
            TE.wait_ge(aQev.h, aQev.val(qe0 + g - 4))
            if g < H:
                for k in range(NCD):
                    TE.matmul(bank[:, :], swq[:, k, g * 128:(g + 1) * 128],
                              sxln[:, k, :], start=(k == 0), stop=(k == NCD - 1))
            else:
                lc = g - H
                for k in range(NCD):
                    TE.matmul(bank[:, :], swd[:, k, lc * 128:(lc + 1) * 128],
                              sxln[:, k, :], start=(k == 0), stop=(k == NCD - 1))
            tQg.inc(TE.q[-1])
            SC.wait_ge(tQg.h, tQg.n)
            if g < H:
                SC.activation(sqstage[:, g, :], bank[:, :], AF.Copy)
            else:
                SC.activation(sckv[:, g - H, ts], bank[:, :], AF.Copy)
            aQev.inc(SC.q[-1])
        SP.wait_ge(aQev.h, aQev.val(qe0 + H - 1))
        SP.dma_start(out=q_scr.rearrange("(h p) t -> p h t", p=128)[:, :, ts],
                     in_=sqstage[:])
        sQst.inc(SP.q[-1])
    SP.wait_ge(sQst.h, sQst.n)
    barrier()
    p0.close()

    p1 = ExitStack()
    sK = sbuf(p1, "sK", (128, S), BF16)
    sV = sbuf(p1, "sV", (128, NCH, 128), BF16)
    sQh = sbuf(p1, "sQh", (128, 2, S), BF16)
    spT = sbuf(p1, "spT", (128, NCH, TT), BF16)
    so_st = sbuf(p1, "so_st", (128, 2, S), BF16)
    swuk = sbuf(p1, "swuk", (128, 2, 4, 128), BF16)
    swuv = sbuf(p1, "swuv", (128, 2, 4, 128), BF16)
    srecip = sbuf(p1, "srecip", (1, TT), F32)
    srb = sbuf(p1, "srb", (128, TT), F32)

    sKV = dsem("sKV"); sQl = dsem("sQl"); sOst = dsem("sOst")
    tKV = role("tKV", TE); aKe = role("aKe", SC); dVe = role("dVe", DV)
    tSc = role("tSc", TE); aPt = role("aPt", SC)
    tPV = role("tPV", TE); tL = role("tL", TE)
    dRec = role("dRec", DV); tRb = role("tRb", TE); aRb = role("aRb", SC)
    dO = role("dO", DV)

    for h in range(H):
        sl = h % 2
        if h >= 2:
            PO.wait_ge(tKV.h, tKV.val((h - 1) * 40 - 1))
            SP.wait_ge(tSc.h, tSc.val((h - 1) * 256 - 1))
        PO.dma_start(out=swuk[:, sl],
                     in_=wukT.rearrange("h (c p) d -> h p c d", p=128)[h])
        sKV.inc(PO.q[-1])
        PO.dma_start(out=swuv[:, sl],
                     in_=wuvT.rearrange("h (c p) d -> h p c d", p=128)[h])
        sKV.inc(PO.q[-1])
        SP.dma_start(out=sQh[:, sl], in_=q_scr[h * 128:(h + 1) * 128, :])
        sQl.inc(SP.q[-1])

        TE.wait_ge(sKV.h, sKV.n)
        if h > 0:
            TE.wait_ge(dO.h, dO.val(h * 8 - 1))
            TE.wait_ge(aPt.h, aPt.n)
        ke0 = aKe.count(); ve0 = dVe.count()
        for g in range(40):
            bank = banks[g % 2]
            if g >= 2:
                pg = g - 2
                if pg < 8:
                    TE.wait_ge(aKe.h, aKe.val(ke0 + pg))
                else:
                    TE.wait_ge(dVe.h, dVe.val(ve0 + pg - 8))
            if g < 8:
                tc = slice(g * 512, (g + 1) * 512)
                for lc in range(4):
                    TE.matmul(bank[:, :], swuk[:, sl, lc, :], sckv[:, lc, tc],
                              start=(lc == 0), stop=(lc == 3))
                tKV.inc(TE.q[-1])
                SC.wait_ge(tKV.h, tKV.n)
                SC.activation(sK[:, g * 512:(g + 1) * 512], bank[:, :], AF.Copy)
                aKe.inc(SC.q[-1])
            else:
                v = g - 8
                tc = slice(v * 128, (v + 1) * 128)
                for lc in range(4):
                    TE.matmul(bank[:, 0:128], sckv[:, lc, tc], swuv[:, sl, lc, :],
                              start=(lc == 0), stop=(lc == 3))
                tKV.inc(TE.q[-1])
                DV.wait_ge(tKV.h, tKV.n)
                DV.tensor_copy(sV[:, v, :], bank[:, 0:128])
                dVe.inc(DV.q[-1])

        TE.wait_ge(sQl.h, sQl.n)
        for sq in range(8):
            sqs = slice(sq * 512, (sq + 1) * 512)
            if h >= 2 and sq == 0:
                DV.wait_ge(sOst.h, sOst.val(h - 2))
            TE.wait_ge(dRec.h, dRec.n)
            TE.wait_ge(dO.h, dO.n)
            TE.wait_ge(aRb.h, aRb.n)
            pt0 = aPt.count()
            for tc in range(NCH + 1):
                if tc < NCH:
                    bank = banks[tc % 4]
                    if tc >= 4:
                        TE.wait_ge(aPt.h, aPt.val(pt0 + tc - 4))
                    if sq == 0:
                        TE.wait_ge(aKe.h, aKe.val(ke0 + tc // 4))
                    TE.matmul(bank[:, :], sK[:, tc * 128:(tc + 1) * 128],
                              sQh[:, sl, sqs], start=True, stop=True,
                              skip_group_check=True)
                    tSc.inc(TE.q[-1])
                    SC.wait_ge(tSc.h, tSc.n)
                    SC.activation(spT[:, tc, :], bank[:, :], AF.Exp,
                                  bias=kfc[:, 2:3], scale=ISQ)
                    aPt.inc(SC.q[-1])
                if tc >= 1:
                    pc = tc - 1
                    TE.wait_ge(aPt.h, aPt.val(pt0 + pc))
                    if sq == 0:
                        TE.wait_ge(dVe.h, dVe.val(ve0 + pc))
                    TE.matmul(banks[4][0:1, :], k_ones_col[:], spT[:, pc, :],
                              start=(pc == 0), stop=(pc == NCH - 1),
                              skip_group_check=True)
                    tL.inc(TE.q[-1])
                    TE.matmul(banks[5][:, :], sV[:, pc, :], spT[:, pc, :],
                              start=(pc == 0), stop=(pc == NCH - 1),
                              skip_group_check=True)
                    tPV.inc(TE.q[-1])
            DV.wait_ge(tL.h, tL.n)
            DV.reciprocal(srecip[:], banks[4][0:1, :])
            dRec.inc(DV.q[-1])
            TE.wait_ge(dRec.h, dRec.n)
            TE.matmul(banks[6][:, :], k_ones_row_f[:], srecip[:], start=True, stop=True)
            tRb.inc(TE.q[-1])
            SC.wait_ge(tRb.h, tRb.n)
            SC.activation(srb[:], banks[6][:, :], AF.Copy)
            aRb.inc(SC.q[-1])
            DV.wait_ge(aRb.h, aRb.n)
            DV.wait_ge(tPV.h, tPV.n)
            DV.tensor_tensor(so_st[:, sl, sqs], banks[5][:, :], srb[:], ALU.mult)
            dO.inc(DV.q[-1])
        SP.wait_ge(dO.h, dO.n)
        SP.dma_start(out=o_scr[h * 128:(h + 1) * 128, :], in_=so_st[:, sl])
        sOst.inc(SP.q[-1])
    SP.wait_ge(sOst.h, sOst.n)
    barrier()
    p1.close()
    p01.close()

    p2 = ExitStack()
    swo = sbuf(p2, "swo", (128, NCD, DM), BF16)
    sxt = sbuf(p2, "sxt", (128, 2, NCD, TT), F32)
    sot = sbuf(p2, "sot", (128, NCD, TT), BF16)
    sx1 = sbuf(p2, "sx1", (128, NCD, TT), F32)

    sWo = dsem("sWo"); sX2l = dsem("sX2l"); sO2l = dsem("sO2l"); sX1st = dsem("sX1st")
    tWo = role("tWo", TE); dX1 = role("dX1", DV)

    PO.dma_start(out=swo[:], in_=woT.rearrange("(c p) j -> p c j", p=128))
    sWo.inc(PO.q[-1])
    TE.wait_ge(sWo.h, sWo.n)
    for tt in range(NTT):
        sl = tt % 2
        ts = slice(tt * TT, (tt + 1) * TT)
        if tt >= 2:
            SP.wait_ge(dX1.h, dX1.val((tt - 1) * NCD - 1))
        SP.dma_start(out=sxt[:, sl],
                     in_=x_in.rearrange("(c p) t -> p c t", p=128)[:, :, ts])
        sX2l.inc(SP.q[-1])
        if tt >= 1:
            SP.wait_ge(tWo.h, tWo.val(tt * NCD - 1))
        SP.dma_start(out=sot[:], in_=o_scr.rearrange("(c p) t -> p c t", p=128)[:, :, ts])
        sO2l.inc(SP.q[-1])
        TE.wait_ge(sO2l.h, sO2l.n)
        DV.wait_ge(sX2l.h, sX2l.n)
        if tt >= 1:
            DV.wait_ge(sX1st.h, sX1st.val(tt - 1))
        x10 = dX1.count()
        for do in range(NCD):
            bank = banks[do % 4]
            TE.wait_ge(dX1.h, dX1.val(x10 + do - 4))
            for k in range(NCD):
                TE.matmul(bank[:, :], swo[:, k, do * 128:(do + 1) * 128],
                          sot[:, k, :], start=(k == 0), stop=(k == NCD - 1))
            tWo.inc(TE.q[-1])
            DV.wait_ge(tWo.h, tWo.n)
            DV.tensor_tensor(sx1[:, do, :], bank[:, :], sxt[:, sl, do, :], ALU.add)
            dX1.inc(DV.q[-1])
        SP.wait_ge(dX1.h, dX1.n)
        SP.dma_start(out=x1_scr.rearrange("(c p) t -> p c t", p=128)[:, :, ts],
                     in_=sx1[:])
        sX1st.inc(SP.q[-1])
    SP.wait_ge(sX1st.h, sX1st.n)
    barrier()
    p2.close()

    def emit_ffn(x_src, x_dst, tag):
        pa = ExitStack()
        fxb = sbuf(pa, f"fxb{tag}", (128, 2, NCD, TT), BF16)
        fxq = sbuf(pa, f"fxq{tag}", (128, NCD, TT), BF16)
        fxln = sbuf(pa, f"fxln{tag}", (128, 2, NCD, TT), BF16)
        frows = sbuf(pa, f"frows{tag}", (1, 3, TT), F32)
        sFx = dsem("sFx"); sFst = dsem("sFst")
        fx0 = sFst.count()
        ln0 = dXln.count()
        for tt in range(NTT):
            sl = tt % 2
            ts = slice(tt * TT, (tt + 1) * TT)
            if tt >= 2:
                PO.wait_ge(dXln.h, dXln.val(ln0 + tt - 2))
            PO.dma_start(out=fxb[:, sl],
                         in_=x_src.rearrange("(c p) t -> p c t", p=128)[:, :, ts])
            sFx.inc(PO.q[-1])
            SC.wait_ge(sFx.h, sFx.n)
            TE.wait_ge(sFx.h, sFx.n)
            if tt > 0:
                TE.wait_ge(dRow.h, dRow.n)
                TE.wait_ge(dXln.h, dXln.n)
                SC.wait_ge(tStat.h, tStat.n)
            if tt >= 2:
                DV.wait_ge(sFst.h, sFst.val(fx0 + tt - 2))
            emit_ln(fxb[:, sl], fxq, fxln[:, sl],
                    (frows[:, 0, :], frows[:, 1, :], frows[:, 2, :]))
            SP.wait_ge(dXln.h, dXln.n)
            SP.dma_start(out=xln_scr.rearrange("(c p) t -> p c t", p=128)[:, :, ts],
                         in_=fxln[:, sl])
            sFst.inc(SP.q[-1])
        barrier()
        pa.close()

        pb = ExitStack()
        HB = 1408; NHB = 4; HBC = 11
        fgw = sbuf(pb, f"fgw{tag}", (128, NCD, HB), BF16)
        fuw = sbuf(pb, f"fuw{tag}", (128, NCD, HB), BF16)
        fxt = sbuf(pb, f"fxt{tag}", (128, 2, NCD, TT), BF16)
        fg = sbuf(pb, f"fg{tag}", (128, HBC, TT), BF16)
        fgu = sbuf(pb, f"fgu{tag}", (128, 2, HBC, TT), BF16)
        ksg = sbuf(pb, f"ksg{tag}", (128, NHC), F32)
        ksu = sbuf(pb, f"ksu{tag}", (128, NHC), F32)
        sGW = dsem("sGW"); sXl = dsem("sXl"); sGUst = dsem("sGUst")
        tG = role("tG", TE); aG = role("aG", SC)
        tU = role("tU", TE); dGU = role("dGU", DV); dSub = role("dSub", DV)
        SP.dma_start(out=ksg[:], in_=sg_c)
        sGW.inc(SP.q[-1])
        SP.dma_start(out=ksu[:], in_=su_c)
        sGW.inc(SP.q[-1])
        gust0 = sGUst.count()
        tg0 = tG.count(); tu0 = tU.count()
        for hb in range(NHB):
            hs = slice(hb * HB, (hb + 1) * HB)
            if hb > 0:
                PO.wait_ge(tG.h, tG.n)
                PO.wait_ge(tU.h, tU.n)
            PO.dma_start(out=fgw[:],
                         in_=g_int.rearrange("(c p) j -> p c j", p=128)[:, :, hs])
            sGW.inc(PO.q[-1])
            PO.dma_start(out=fuw[:],
                         in_=u_int.rearrange("(c p) j -> p c j", p=128)[:, :, hs])
            sGW.inc(PO.q[-1])
            DV.wait_ge(sGW.h, sGW.n)
            DV.tensor_scalar_add(fgw[:], fgw[:], -128.0)
            DV.tensor_scalar_add(fuw[:], fuw[:], -128.0)
            dSub.inc(DV.q[-1])
            TE.wait_ge(dSub.h, dSub.n)
            for tt in range(NTT):
                sl = tt % 2
                ts = slice(tt * TT, (tt + 1) * TT)
                it = hb * NTT + tt
                if it >= 2:
                    SP.wait_ge(tG.h, tG.val(tg0 + (it - 1) * HBC - 1))
                    SP.wait_ge(tU.h, tU.val(tu0 + (it - 1) * HBC - 1))
                SP.dma_start(out=fxt[:, sl],
                             in_=xln_scr.rearrange("(c p) t -> p c t", p=128)[:, :, ts])
                sXl.inc(SP.q[-1])
                TE.wait_ge(sXl.h, sXl.n)
                if it >= 2:
                    DV.wait_ge(sGUst.h, sGUst.val(gust0 + it - 2))
                for hc in range(HBC):
                    gbank = banks[(0, 1, 4, 5)[hc % 4]]
                    ubank = banks[(2, 3, 6, 7)[hc % 4]]
                    TE.wait_ge(aG.h, aG.val(aG.count() - 4))
                    for k in range(NCD):
                        TE.matmul(gbank[:, :], fgw[:, k, hc * 128:(hc + 1) * 128],
                                  fxt[:, sl, k, :], start=(k == 0), stop=(k == NCD - 1))
                    tG.inc(TE.q[-1])
                    TE.wait_ge(dGU.h, dGU.val(dGU.count() - 4))
                    for k in range(NCD):
                        TE.matmul(ubank[:, :], fuw[:, k, hc * 128:(hc + 1) * 128],
                                  fxt[:, sl, k, :], start=(k == 0), stop=(k == NCD - 1))
                    tU.inc(TE.q[-1])
                    SC.wait_ge(tG.h, tG.n)
                    SC.activation(fg[:, hc, :], gbank[:, :], AF.Silu,
                                  scale=ksg[:, hb * HBC + hc:hb * HBC + hc + 1])
                    aG.inc(SC.q[-1])
                    DV.wait_ge(tU.h, tU.n)
                    DV.wait_ge(aG.h, aG.n)
                    DV.scalar_tensor_tensor(fgu[:, sl, hc, :], ubank[:, :],
                                            ksu[:, hb * HBC + hc:hb * HBC + hc + 1],
                                            fg[:, hc, :], ALU.mult, ALU.mult)
                    dGU.inc(DV.q[-1])
                SP.wait_ge(dGU.h, dGU.n)
                SP.dma_start(out=gu_scr.rearrange("(c p) t -> p c t", p=128)
                             [:, hb * HBC:(hb + 1) * HBC, ts], in_=fgu[:, sl])
                sGUst.inc(SP.q[-1])
        barrier()
        pb.close()

        pc = ExitStack()
        DOB = 512; NDOB = 4; DBC = 4
        fdw = sbuf(pc, f"fdw{tag}", (128, NHC, DOB), BF16)
        fgut = sbuf(pc, f"fgut{tag}", (128, 2, NHC, TT), BF16)
        fxr = sbuf(pc, f"fxr{tag}", (128, 2, DBC, TT), F32)
        fxo = sbuf(pc, f"fxo{tag}", (128, DBC, TT), F32)
        ksd = sbuf(pc, f"ksd{tag}", (128, NCD), F32)
        sDW = dsem("sDW"); sGUl = dsem("sGUl"); sXr = dsem("sXr"); sXost = dsem("sXost")
        tD = role("tD", TE); dXo = role("dXo", DV)
        SP.dma_start(out=ksd[:], in_=sd_c)
        sDW.inc(SP.q[-1])
        td0 = tD.count(); xst0 = sXost.count()
        for db in range(NDOB):
            ds = slice(db * DOB, (db + 1) * DOB)
            if db > 0:
                PO.wait_ge(tD.h, tD.n)
            PO.dma_start(out=fdw[:],
                         in_=d_int.rearrange("(c p) j -> p c j", p=128)[:, :, ds])
            sDW.inc(PO.q[-1])
            DV.wait_ge(sDW.h, sDW.n)
            DV.tensor_scalar_add(fdw[:], fdw[:], -128.0)
            dSub.inc(DV.q[-1])
            TE.wait_ge(dSub.h, dSub.n)
            tdb0 = tD.count()
            for tt in range(NTT):
                sl = tt % 2
                ts = slice(tt * TT, (tt + 1) * TT)
                it = db * NTT + tt
                if tt >= 2:
                    SP.wait_ge(tD.h, tD.val(tdb0 + (tt - 1) * DBC - 1))
                SP.dma_start(out=fgut[:, sl],
                             in_=gu_scr.rearrange("(c p) t -> p c t", p=128)[:, :, ts])
                sGUl.inc(SP.q[-1])
                SP.dma_start(out=fxr[:, sl],
                             in_=x_src.rearrange("(c p) t -> p c t", p=128)
                             [:, db * DBC:(db + 1) * DBC, ts])
                sXr.inc(SP.q[-1])
                TE.wait_ge(sGUl.h, sGUl.n)
                DV.wait_ge(sXr.h, sXr.n)
                if it >= 1:
                    DV.wait_ge(sXost.h, sXost.val(xst0 + it - 1))
                for dc in range(DBC):
                    bank = banks[4 + dc % 4]
                    TE.wait_ge(dXo.h, dXo.val(dXo.count() - 4))
                    for k in range(NHC):
                        TE.matmul(bank[:, :], fdw[:, k, dc * 128:(dc + 1) * 128],
                                  fgut[:, sl, k, :], start=(k == 0), stop=(k == NHC - 1))
                    tD.inc(TE.q[-1])
                    DV.wait_ge(tD.h, tD.n)
                    DV.scalar_tensor_tensor(fxo[:, dc, :], bank[:, :],
                                            ksd[:, db * DBC + dc:db * DBC + dc + 1],
                                            fxr[:, sl, dc, :], ALU.mult, ALU.add)
                    dXo.inc(DV.q[-1])
                SP.wait_ge(dXo.h, dXo.n)
                SP.dma_start(out=x_dst.rearrange("(c p) t -> p c t", p=128)
                             [:, db * DBC:(db + 1) * DBC, ts], in_=fxo[:])
                sXost.inc(SP.q[-1])
        SP.wait_ge(sXost.h, sXost.n)
        barrier()
        pc.close()

    def emit_mamba_phase():
        pa = ExitStack()
        mxb = sbuf(pa, "mxb", (128, NCD, TT), BF16)
        mxq = sbuf(pa, "mxq", (128, NCD, TT), BF16)
        mln3 = sbuf(pa, "mln3", (128, NCD, S), BF16)
        mrows = sbuf(pa, "mrows", (1, 3, TT), F32)
        swin = sbuf(pa, "swin", (128, 2, NCD, 128), BF16)
        zst = sbuf(pa, "zst", (128, 2, S), BF16)
        dtst = sbuf(pa, "dtst", (64, 2, TT), F32)
        sMx = dsem("sMx"); sWin = dsem("sWin"); sZst = dsem("sZst")
        tZg = role("tZg", TE); aZe = role("aZe", SC)
        ln0 = dXln.count()
        for tt in range(NTT):
            ts = slice(tt * TT, (tt + 1) * TT)
            if tt >= 1:
                PO.wait_ge(dXln.h, dXln.val(ln0 + tt - 1))
            PO.dma_start(out=mxb[:],
                         in_=x2_scr.rearrange("(c p) t -> p c t", p=128)[:, :, ts])
            sMx.inc(PO.q[-1])
            SC.wait_ge(sMx.h, sMx.n)
            TE.wait_ge(sMx.h, sMx.n)
            if tt > 0:
                TE.wait_ge(dRow.h, dRow.n)
                TE.wait_ge(dXln.h, dXln.n)
                SC.wait_ge(tStat.h, tStat.n)
            emit_ln(mxb[:], mxq, mln3[:, :, ts],
                    (mrows[:, 0, :], mrows[:, 1, :], mrows[:, 2, :]))
        TE.wait_ge(dXln.h, dXln.n)
        zg0 = tZg.count(); ze0 = aZe.count()
        for oc in range(66):
            slw = oc % 2
            W = 128 if oc < 65 else 64
            if oc >= 2:
                PO.wait_ge(tZg.h, tZg.val(zg0 + (oc - 1) * NTT - 1))
            PO.dma_start(out=swin[:, slw, :, 0:W],
                         in_=winT.rearrange("(c p) j -> p c j", p=128)
                         [:, :, oc * 128:oc * 128 + W])
            sWin.inc(PO.q[-1])
            TE.wait_ge(sWin.h, sWin.n)
            if oc >= 2 and oc < 65:
                SC.wait_ge(sZst.h, sZst.val(-2))
            for tt in range(NTT):
                ts = slice(tt * TT, (tt + 1) * TT)
                gi = oc * NTT + tt
                bank = banks[4 + gi % 4]
                TE.wait_ge(aZe.h, aZe.val(ze0 + gi - 4))
                for k in range(NCD):
                    TE.matmul(bank[0:W, :], swin[:, slw, k, 0:W], mln3[:, k, ts],
                              start=(k == 0), stop=(k == NCD - 1))
                tZg.inc(TE.q[-1])
                SC.wait_ge(tZg.h, tZg.n)
                if oc < 65:
                    SC.activation(zst[:, slw, ts], bank[:, :], AF.Copy)
                    aZe.inc(SC.q[-1])
                else:
                    if tt >= 2:
                        SC.wait_ge(sZst.h, sZst.val(-2))
                    SC.activation(dtst[:, tt % 2, :], bank[0:64, :], AF.Copy)
                    aZe.inc(SC.q[-1])
                    SP.wait_ge(aZe.h, aZe.n)
                    SP.dma_start(out=dtr_scr[:, ts], in_=dtst[:, tt % 2, :])
                    sZst.inc(SP.q[-1])
            if oc < 65:
                SP.wait_ge(aZe.h, aZe.n)
                SP.dma_start(out=zx_scr[oc * 128:(oc + 1) * 128, :], in_=zst[:, slw])
                sZst.inc(SP.q[-1])
        SP.wait_ge(sZst.h, sZst.n)
        barrier()
        pa.close()

        pb = ExitStack()
        k_ltbig = sbuf(pb, "k_ltbig", (128, 128), BF16)
        k_i4 = sbuf(pb, "k_i4", (128, 512), BF16)
        k_e128 = sbuf(pb, "k_e128", (64, 64, 128), BF16)
        k_e64 = sbuf(pb, "k_e64", (64, 32, 128), BF16)
        k_o64_128 = sbuf(pb, "k_o64_128", (64, 128), BF16)
        k_o64_64 = sbuf(pb, "k_o64_64", (64, 64), BF16)
        k_of64 = sbuf(pb, "k_of64", (64, 128), F32)
        k_par = sbuf(pb, "k_par", (64, 128), BF16)
        k_pm = sbuf(pb, "k_pm", (64, 32, 128), BF16)
        k_cw = sbuf(pb, "k_cw", (128, 33, 4), F32)
        k_cb = sbuf(pb, "k_cb", (128, 33), F32)
        k_dD = sbuf(pb, "k_dD", (128, NCH), F32)
        k_nw = sbuf(pb, "k_nw", (128, 32), F32)
        k_dtb = sbuf(pb, "k_dtb", (64, 1), F32)
        k_A = sbuf(pb, "k_A", (64, 1), F32)
        sK4 = dsem("sK4")
        for dst, src in ((k_ltbig, c_ltbig), (k_i4, c_i4),
                         (k_e128, c_e128.rearrange("h (a t) -> h a t", t=128)),
                         (k_e64, c_e64.rearrange("h (a t) -> h a t", t=128)),
                         (k_o64_128, c_ones64_128), (k_o64_64, c_ones64_64),
                         (k_par, c_parity),
                         (k_pm, c_pairmask.rearrange("h (a t) -> h a t", t=128)),
                         (k_cw, convw_c), (k_cb, convb_c), (k_dD, dD_big),
                         (k_nw, normw_c), (k_dtb, dtb_col), (k_A, alog_col)):
            SP.dma_start(out=dst[:], in_=src)
            sK4.inc(SP.q[-1])
        for q in (SC, DV, PO, TE):
            q.wait_ge(sK4.h, sK4.n)
        SC.activation(k_A[:], k_A[:], AF.Exp)
        aA0 = role("aA0", SC); aA0.inc(SC.q[-1])
        DV.wait_ge(aA0.h, aA0.n)
        DV.tensor_scalar_mul(k_A[:], k_A[:], -1.0)
        dA0 = role("dA0", DV); dA0.inc(DV.q[-1])
        PO.memset(k_of64[:], 1.0)

        mxbc = sbuf(pb, "mxbc", (128, 2, 33, 131), BF16)
        mcv = sbuf(pb, "mcv", (128, 2, 17, 128), BF16)
        mxc = sbuf(pb, "mxc", (128, 2, 33, 128), BF16)
        zbuf = sbuf(pb, "zbuf", (128, NCH, 128), BF16)
        mzs = sbuf(pb, "mzs", (128, NCH, 128), BF16)
        mXtm = sbuf(pb, "mXtm", (128, NCH, 128), BF16)
        mMT = sbuf(pb, "mMT", (128, 32, 128), BF16)
        mclE = sbuf(pb, "mclE", (64, 32, 128), BF16)
        mPLb = sbuf(pb, "mPLb", (64, 32, 128), BF16)
        mPtP = sbuf(pb, "mPtP", (128, NCH, 128), BF16)
        mS = sbuf(pb, "mS", (64, 4096), F32)
        mSb = sbuf(pb, "mSb", (64, 4096), BF16)
        mPLs = sbuf(pb, "mPLs", (64, 4096), BF16)
        my = sbuf(pb, "my", (128, NCH, 128), BF16)
        myq = sbuf(pb, "myq", (128, NCH, 128), BF16)
        mG0 = sbuf(pb, "mG0", (128, 128), BF16)
        mBCt = sbuf(pb, "mBCt", (128, 128), BF16)
        mC0 = sbuf(pb, "mC0", (64, 2, 128), BF16)
        mPm = sbuf(pb, "mPm", (64, 32, 128), BF16)
        mdtr = sbuf(pb, "mdtr", (64, 2, 128), F32)
        msc = sbuf(pb, "msc", (64, 2, 5, 128), F32)
        mscb = sbuf(pb, "mscb", (64, 2, 4, 128), BF16)
        mPL = sbuf(pb, "mPL", (64, 2, 1), F32)
        me1t = sbuf(pb, "me1t", (128, 512), F32)
        me2t = sbuf(pb, "me2t", (128, 512), F32)
        mexp = sbuf(pb, "mexp", (128, 2, 512), BF16)
        mrr = sbuf(pb, "mrr", (1, 128), F32)
        mrb = sbuf(pb, "mrb", (128, 128), F32)

        PO.memset(mS[:], 0.0)
        pS0 = role("pS0", PO); pS0.inc(PO.q[-1])
        DV.wait_ge(pS0.h, pS0.n)

        sZl = dsem("sZl"); sXBl = dsem("sXBl"); sDtl = dsem("sDtl"); sYn = dsem("sYn")
        dCv = role("dCv", DV); aSl = role("aSl", SC)
        dDt = role("dDt", DV); aDt = role("aDt", SC); aP4 = role("aP4", SC)
        dFr = role("dFr", DV)
        pCl = role("pCl", PO)
        tTr = role("tTr", TE); dTr = role("dTr", DV)
        tG0 = role("tG0", TE); dG0 = role("dG0", DV)
        tDc = role("tDc", TE); aDc = role("aDc", SC); pMT = role("pMT", PO)
        tPt = role("tPt", TE); aPt4 = role("aPt4", SC)
        pSb = role("pSb", PO)
        tY = role("tY", TE); dY1 = role("dY1", DV); pY = role("pY", PO)
        aZs = role("aZs", SC)
        aY2 = role("aY2", SC); tRm = role("tRm", TE); dRm = role("dRm", DV)
        aRm = role("aRm", SC); tRb4 = role("tRb4", TE); aRb4 = role("aRb4", SC)
        pYn = role("pYn", PO)
        tXp = role("tXp", TE); dXp = role("dXp", DV)
        tSt = role("tSt", TE); tPl = role("tPl", TE); aPl = role("aPl", SC)
        dSt = role("dSt", DV)

        def front(ck):
            t0 = ck * CH
            sl = ck % 2
            if ck >= 2:
                SP.wait_ge(dCv.h, dCv.val(-3))
            if ck == 0:
                PO.memset(mxbc[:, 0, :, 0:3], 0.0)
                pCl.inc(PO.q[-1])
                SP.wait_ge(pCl.h, pCl.n)
                SP.dma_start(out=mxbc[:, 0, :, 3:131],
                             in_=zx_scr[4096:8320, 0:128]
                             .rearrange("(c p) t -> p c t", p=128))
            else:
                SP.dma_start(out=mxbc[:, sl],
                             in_=zx_scr[4096:8320, t0 - 3:t0 + 128]
                             .rearrange("(c p) t -> p c t", p=128))
            sXBl.inc(SP.q[-1])
            if ck >= 2:
                SP.wait_ge(dDt.h, dDt.val(-8))
            SP.dma_start(out=mdtr[:, sl, :], in_=dtr_scr[:, t0:t0 + 128])
            sDtl.inc(SP.q[-1])

            DV.wait_ge(sXBl.h, sXBl.n)
            SC.wait_ge(sXBl.h, sXBl.n)
            if ck >= 2:
                SC.wait_ge(pY.h, pY.n)
                SC.wait_ge(tTr.h, tTr.n)
                SC.wait_ge(tY.h, tY.n)
            for half in range(2):
                c0, c1 = (0, 17) if half == 0 else (17, 33)
                ncc = c1 - c0
                if ck >= 1:
                    DV.wait_ge(aSl.h, aSl.val(-2 + half))
                for c in range(c0, c1):
                    cc = c - c0
                    DV.tensor_scalar(mcv[:, half, cc, :], mxbc[:, sl, c, 0:128],
                                     k_cw[:, c, 0:1], k_cb[:, c:c + 1],
                                     ALU.mult, ALU.add)
                    for tap in (1, 2, 3):
                        DV.scalar_tensor_tensor(mcv[:, half, cc, :],
                                                mxbc[:, sl, c, tap:tap + 128],
                                                k_cw[:, c, tap:tap + 1],
                                                mcv[:, half, cc, :],
                                                ALU.mult, ALU.add)
                dCv.inc(DV.q[-1])
                SC.wait_ge(dCv.h, dCv.n)
                SC.activation(mxc[:, sl, c0:c1, :], mcv[:, half, 0:ncc, :], AF.Silu)
                aSl.inc(SC.q[-1])

            a_r = msc[:, sl, 0, :]; t_r = msc[:, sl, 1, :]
            dt_r = msc[:, sl, 2, :]; lndt_r = msc[:, sl, 3, :]
            cl_r = msc[:, sl, 4, :]; wdec_r = msc[:, sl, 1, :]
            DV.wait_ge(sDtl.h, sDtl.n)
            if ck == 0:
                DV.wait_ge(dA0.h, dA0.n)
            if ck >= 2:
                DV.wait_ge(pCl.h, pCl.n)
                DV.wait_ge(tSt.h, tSt.n)
            DV.tensor_scalar_add(a_r, mdtr[:, sl, :], k_dtb[:, 0:1])
            dDt.inc(DV.q[-1])
            SC.wait_ge(dDt.h, dDt.n)
            SC.activation(t_r, a_r, AF.Abs)
            SC.activation(t_r, t_r, AF.Exp, scale=-1.0)
            SC.activation(t_r, t_r, AF.Ln, bias=kfc[0:64, 1:2])
            aDt.inc(SC.q[-1])
            DV.wait_ge(aDt.h, aDt.n)
            DV.tensor_scalar_max(dt_r, a_r, 0.0)
            DV.tensor_tensor(dt_r, dt_r, t_r, ALU.add)
            dDt.inc(DV.q[-1])
            SC.wait_ge(dDt.h, dDt.n)
            SC.activation(lndt_r, dt_r, AF.Ln)
            aDt.inc(SC.q[-1])
            DV.tensor_scalar_mul(t_r, dt_r, k_A[:, 0:1])
            DV.tensor_tensor_scan(cl_r, k_of64[:], t_r, 0.0, ALU.mult, ALU.add)
            DV.tensor_copy(mscb[:, sl, 0, :], cl_r)
            dDt.inc(DV.q[-1])
            SC.wait_ge(dDt.h, dDt.n)
            SC.activation(mscb[:, sl, 2, :], cl_r, AF.Exp)
            SC.activation(mPL[:, sl, :], cl_r[:, 127:128], AF.Exp)
            aP4.inc(SC.q[-1])
            DV.wait_ge(aP4.h, aP4.n)
            DV.tensor_tensor(mscb[:, sl, 1, :], lndt_r, cl_r, ALU.subtract)
            DV.tensor_scalar(wdec_r, cl_r, cl_r[:, 127:128], None, ALU.subtract)
            dDt.inc(DV.q[-1])
            SC.wait_ge(dDt.h, dDt.n)
            SC.activation(mscb[:, sl, 3, :], wdec_r, AF.Exp, scale=-1.0)
            aDt.inc(SC.q[-1])
            DV.wait_ge(aDt.h, aDt.n)
            DV.tensor_tensor(mscb[:, sl, 3, :], mscb[:, sl, 3, :], dt_r, ALU.mult)
            DV.tensor_copy(mC0[:, sl, :], mxc[64:128, sl, 32, :])
            dFr.inc(DV.q[-1])

        def back(ck):
            t0 = ck * CH
            sl = ck % 2
            ap0 = mscb[:].ap[0][0]
            rep_cl = bass.AP(mscb, sl * 4 * 128 + 0 * 128, [[ap0, 64], [0, 32], [1, 128]])
            rep_P = bass.AP(mscb, sl * 4 * 128 + 2 * 128, [[ap0, 64], [0, 32], [1, 128]])
            if ck > 0:
                SP.wait_ge(aZs.h, aZs.n)
            SP.dma_start(out=zbuf[:], in_=zx_scr[0:4096, t0:t0 + 128]
                         .rearrange("(c p) t -> p c t", p=128))
            sZl.inc(SP.q[-1])
            PO.wait_ge(dFr.h, dFr.n)
            if ck > 0:
                PO.wait_ge(tDc.h, tDc.n)
                PO.wait_ge(tPl.h, tPl.n)
                PO.wait_ge(tPt.h, tPt.n)
                PO.wait_ge(sYn.h, sYn.n)
            PO.tensor_tensor(mclE[:], k_e128[:, 0:32, :], rep_cl, ALU.mult)
            PO.tensor_tensor(mPm[:], k_pm[:], rep_P, ALU.mult)
            pCl.inc(PO.q[-1])
            DV.wait_ge(dFr.h, dFr.n)
            if ck > 0:
                DV.wait_ge(tPl.h, tPl.n)
            DV.tensor_scalar(mPLb[:], k_e64[:], mPL[:, sl, 0:1], None, ALU.mult)
            dPlm = role("dPlm", DV); dPlm.inc(DV.q[-1])
            SC.wait_ge(sZl.h, sZl.n)
            if ck > 0:
                SC.wait_ge(pY.h, pY.n)
            SC.activation(mzs[:], zbuf[:], AF.Silu)
            aZs.inc(SC.q[-1])
            TE.wait_ge(dFr.h, dFr.n)
            if ck > 0:
                TE.wait_ge(dSt.h, dSt.n)
                TE.wait_ge(dTr.h, dTr.n)
                TE.wait_ge(aPl.h, aPl.n)
            TE.matmul(banks[6][:, 0:128], mxc[0:64, sl, 32, :], mC0[:, sl, :],
                      start=True, stop=True)
            tG0.inc(TE.q[-1])
            DV.wait_ge(tG0.h, tG0.n)
            if ck > 0:
                DV.wait_ge(pMT.h, pMT.n)
            DV.tensor_copy(mG0[:], banks[6][:, 0:128])
            dG0.inc(DV.q[-1])
            TE.wait_ge(aSl.h, aSl.val(-2))
            for cc in range(33):
                bank = banks[6 + cc % 2]
                bb = bank[:, 0:64].bitcast(BF16)
                if cc == 0:
                    TE.wait_ge(dG0.h, dG0.n)
                if cc >= 2:
                    TE.wait_ge(dTr.h, dTr.val(-2))
                TE.transpose(bb, mxc[:, sl, cc, :], k_ident[:])
                tTr.inc(TE.q[-1])
                DV.wait_ge(tTr.h, tTr.n)
                if ck > 0 and cc == 0:
                    DV.wait_ge(tSt.h, tSt.n)
                if cc < 32:
                    DV.tensor_copy(mXtm[:, cc, :], bb)
                else:
                    DV.tensor_copy(mBCt[:], bb)
                dTr.inc(DV.q[-1])
            PO.wait_ge(dFr.h, dFr.n)
            if ck > 0:
                PO.wait_ge(tY.h, tY.n)
            PO.tensor_copy(mSb[:], mS[:])
            pSb.inc(PO.q[-1])
            TE.wait_ge(pCl.h, pCl.n)
            TE.wait_ge(_roles["dPlm"].h, _roles["dPlm"].n)
            for pc in range(8):
                bank = banks[2 + pc % 2]
                if pc >= 2:
                    TE.wait_ge(aPl.h, aPl.val(-2))
                TE.matmul(bank[0:64, :], k_o64_64[:],
                          mPLb[:, pc * 4:(pc + 1) * 4, :].rearrange("p a t -> p (a t)"),
                          start=True, stop=True)
                tPl.inc(TE.q[-1])
                SC.wait_ge(tPl.h, tPl.n)
                if ck > 0 and pc == 0:
                    SC.wait_ge(dSt.h, dSt.n)
                SC.activation(mPLs[:, pc * 512:(pc + 1) * 512], bank[0:64, :], AF.Copy)
                aPl.inc(SC.q[-1])
            for halfd in range(2):
                if halfd == 1:
                    PO.wait_ge(tDc.h, tDc.n)
                    PO.tensor_tensor(mclE[:], k_e128[:, 32:64, :], rep_cl, ALU.mult)
                    pCl.inc(PO.q[-1])
                    TE.wait_ge(pCl.h, pCl.n)
                for pc in range(8):
                    gp = halfd * 8 + pc
                    bank = banks[gp % 2]
                    if gp >= 2:
                        TE.wait_ge(aDc.h, aDc.val(-2))
                    TE.matmul(bank[:, :], k_o64_128[:], mclE[:, pc * 4:(pc + 1) * 4, :],
                              start=True, stop=False)
                    TE.matmul(bank[:, :], mscb[:, sl, 1, :],
                              k_e128[:, gp * 4:(gp + 1) * 4, :],
                              start=False, stop=False)
                    TE.matmul(bank[:, :], k_ltbig[:], k_i4[:], start=False, stop=True)
                    tDc.inc(TE.q[-1])
                    SC.wait_ge(tDc.h, tDc.n)
                    if gp >= 2:
                        SC.wait_ge(pMT.h, pMT.val(-2))
                    SC.activation(mexp[:, gp % 2, :], bank[:, :], AF.Exp)
                    aDc.inc(SC.q[-1])
                    PO.wait_ge(aDc.h, aDc.n)
                    repg = bass.AP(mG0, 0, [[128, 128], [0, 4], [1, 128]])
                    PO.tensor_tensor(mMT[:, pc * 4:(pc + 1) * 4, :],
                                     mexp[:, gp % 2, :].rearrange("p (a t) -> p a t", t=128),
                                     repg, ALU.mult)
                    pMT.inc(PO.q[-1])
                for pq in range(4):
                    ppc = halfd * 4 + pq
                    bank = banks[4 + pq % 2]
                    if ck > 0 or halfd > 0 or pq >= 2:
                        TE.wait_ge(aPt4.h, aPt4.val(-2))
                    TE.wait_ge(pCl.h, pCl.n)
                    TE.matmul(bank[:, :], k_par[:], mPm[:, ppc * 4:(ppc + 1) * 4, :],
                              start=True, stop=True)
                    tPt.inc(TE.q[-1])
                    SC.wait_ge(tPt.h, tPt.n)
                    if ck > 0 and halfd == 0 and pq == 0:
                        SC.wait_ge(dY1.h, dY1.n)
                    SC.activation(mPtP[:, ppc * 4:(ppc + 1) * 4, :]
                                  .rearrange("p a t -> p (a t)"), bank[:, :], AF.Copy)
                    aPt4.inc(SC.q[-1])
                for g2 in range(4):
                    grp = halfd * 4 + g2
                    bYi = banks[2 + grp % 2]
                    bYs = banks[6 + grp % 2]
                    if grp == 0:
                        TE.wait_ge(pSb.h, pSb.n)
                        TE.wait_ge(aZs.h, aZs.n)
                        TE.wait_ge(aPl.h, aPl.n)
                        TE.wait_ge(dTr.h, dTr.n)
                        PO.wait_ge(sYn.h, sYn.n)
                    TE.wait_ge(pMT.h, pMT.val(-1))
                    TE.wait_ge(dY1.h, dY1.val(-2))
                    TE.wait_ge(pY.h, pY.val(-2))
                    TE.wait_ge(aPt4.h, aPt4.n)
                    for pr in range(4):
                        p_ = grp * 4 + pr
                        hh0 = 2 * p_
                        hh1 = hh0 + 1
                        lh0 = hh0 - halfd * 32
                        lh1 = hh1 - halfd * 32
                        TE.matmul(bYi[0:64, pr * 128:(pr + 1) * 128],
                                  mXtm[:, p_, 0:64], mMT[:, lh0, :],
                                  start=True, stop=True)
                        TE.matmul(bYi[64:128, pr * 128:(pr + 1) * 128],
                                  mXtm[:, p_, 64:128], mMT[:, lh1, :],
                                  start=True, stop=True)
                        TE.matmul(bYs[0:64, pr * 128:(pr + 1) * 128],
                                  mSb[:, hh0 * 64:(hh0 + 1) * 64], mC0[:, sl, :],
                                  start=True, stop=True)
                        TE.matmul(bYs[64:128, pr * 128:(pr + 1) * 128],
                                  mSb[:, hh1 * 64:(hh1 + 1) * 64], mC0[:, sl, :],
                                  start=True, stop=True)
                    tY.inc(TE.q[-1])
                    DV.wait_ge(tY.h, tY.n)
                    DV.wait_ge(pY.h, pY.n)
                    DV.tensor_tensor(me1t[:], bYs[:, :],
                                     mPtP[:, grp * 4:(grp + 1) * 4, :]
                                     .rearrange("p a t -> p (a t)"), ALU.mult)
                    DV.tensor_tensor(me2t[:], bYi[:, :], me1t[:], ALU.add)
                    for pr in range(4):
                        p_ = grp * 4 + pr
                        DV.scalar_tensor_tensor(me2t[:, pr * 128:(pr + 1) * 128],
                                                mxc[:, sl, p_, :], k_dD[:, p_:p_ + 1],
                                                me2t[:, pr * 128:(pr + 1) * 128],
                                                ALU.mult, ALU.add)
                    dY1.inc(DV.q[-1])
                    PO.wait_ge(dY1.h, dY1.n)
                    PO.tensor_tensor(my[:, grp * 4:(grp + 1) * 4, :]
                                     .rearrange("p a t -> p (a t)"), me2t[:],
                                     mzs[:, grp * 4:(grp + 1) * 4, :]
                                     .rearrange("p a t -> p (a t)"), ALU.mult)
                    pY.inc(PO.q[-1])

            SC.wait_ge(pY.h, pY.n)
            if ck > 0:
                SC.wait_ge(tRm.h, tRm.n)
                SC.wait_ge(dXp.h, dXp.n)
                SC.wait_ge(tSt.h, tSt.n)
            SC.activation(myq[:], my[:], AF.Square)
            aY2.inc(SC.q[-1])
            TE.wait_ge(aY2.h, aY2.n)
            if ck > 0:
                TE.wait_ge(dRm.h, dRm.n)
            for cc in range(32):
                TE.matmul(banks[4][0:1, 0:128], k_ones_col[:], myq[:, cc, :],
                          start=(cc == 0), stop=(cc == 31))
            tRm.inc(TE.q[-1])
            DV.wait_ge(tRm.h, tRm.n)
            DV.tensor_scalar_mul(mrr[:], banks[4][0:1, 0:128], 1.0 / 4096.0)
            dRm.inc(DV.q[-1])
            SC.wait_ge(dRm.h, dRm.n)
            SC.activation(mrr[:], mrr[:], AF.Ln, bias=kfc[0:1, 0:1])
            SC.activation(mrr[:], mrr[:], AF.Exp, scale=-0.5)
            aRm.inc(SC.q[-1])
            TE.wait_ge(aRm.h, aRm.n)
            if ck > 0:
                TE.wait_ge(aRb4.h, aRb4.n)
            TE.matmul(banks[5][:, 0:128], k_ones_row_f[:], mrr[:], start=True, stop=True)
            tRb4.inc(TE.q[-1])
            SC.wait_ge(tRb4.h, tRb4.n)
            SC.activation(mrb[:], banks[5][:, 0:128], AF.Copy)
            aRb4.inc(SC.q[-1])
            PO.wait_ge(aRb4.h, aRb4.n)
            PO.wait_ge(pY.h, pY.n)
            reprb = bass.AP(mrb, 0, [[128, 128], [0, 32], [1, 128]])
            PO.tensor_tensor(my[:], my[:], reprb, ALU.mult)
            pYn.inc(PO.q[-1])
            DV.wait_ge(pYn.h, pYn.n)
            for cc in range(32):
                DV.tensor_scalar_mul(my[:, cc, :], my[:, cc, :], k_nw[:, cc:cc + 1])
            dYn = role("dYn", DV); dYn.inc(DV.q[-1])
            SP.wait_ge(dYn.h, dYn.n)
            SP.dma_start(out=yn_scr.rearrange("(c p) t -> p c t", p=128)
                         [:, :, t0:t0 + 128], in_=my[:])
            sYn.inc(SP.q[-1])

            TE.wait_ge(dRm.h, dRm.n)
            for pc in range(8):
                bank = banks[2 + pc % 2]
                if pc >= 2:
                    TE.wait_ge(dXp.h, dXp.val(-2))
                TE.matmul(bank[:, :], mscb[:, sl, 3, :],
                          k_e64[:, pc * 4:(pc + 1) * 4, :], start=True, stop=True)
                tXp.inc(TE.q[-1])
                DV.wait_ge(tXp.h, tXp.n)
                DV.tensor_tensor(myq[:, pc * 4:(pc + 1) * 4, :]
                                 .rearrange("p a t -> p (a t)"),
                                 mXtm[:, pc * 4:(pc + 1) * 4, :]
                                 .rearrange("p a t -> p (a t)"), bank[:, :], ALU.mult)
                dXp.inc(DV.q[-1])
            for pc in range(8):
                bS = banks[4 + pc % 2]
                TE.wait_ge(dXp.h, dXp.n)
                if pc < 2:
                    TE.wait_ge(aPl.h, aPl.n)
                if pc >= 2:
                    TE.wait_ge(dSt.h, dSt.val(-2))
                elif ck == 0 and pc == 0:
                    TE.wait_ge(pS0.h, pS0.n)
                TE.matmul(bS[0:64, :], mBCt[:, 0:64],
                          myq[:, pc * 4:(pc + 1) * 4, :].rearrange("p a t -> p (a t)"),
                          start=True, stop=True)
                tSt.inc(TE.q[-1])
                DV.wait_ge(tSt.h, tSt.n)
                DV.wait_ge(pSb.h, pSb.n)
                DV.tensor_tensor(me1t[0:64, :], mS[:, pc * 512:(pc + 1) * 512],
                                 mPLs[:, pc * 512:(pc + 1) * 512], ALU.mult)
                DV.tensor_tensor(mS[:, pc * 512:(pc + 1) * 512], me1t[0:64, :],
                                 bS[0:64, :], ALU.add)
                dSt.inc(DV.q[-1])

        front(0)
        for ck in range(DBG_NCH[0]):
            if ck + 1 < DBG_NCH[0]:
                front(ck + 1)
            back(ck)
            if ck == 0 and DBG_DUMP[0] == 1:
                sDbg = dsem("sDbg")
                for q in ALLQ:
                    q.drain()
                    q.sem_inc(phase.h, 1)
                phase.n += 5
                SP.wait_ge(phase.h, phase.n)
                for dst, src in ((dbg_xc, mxc[:, 0]), (dbg_mt, mMT[:]),
                                 (dbg_xtm, mXtm[:]), (dbg_my, my[:]),
                                 (dbg_ptp, mPtP[:]), (dbg_g0, mG0[:])):
                    SP.dma_start(out=dst, in_=src)
                    sDbg.inc(SP.q[-1])
                SP.wait_ge(sDbg.h, sDbg.n)
                for q in ALLQ:
                    q.wait_ge(sDbg.h, sDbg.n)
        SP.wait_ge(sYn.h, sYn.n)
        barrier()
        pb.close()

        pc4 = ExitStack()
        wwo = sbuf(pc4, "wwo", (128, 32, 1024), BF16)
        wyn = sbuf(pc4, "wyn", (128, 2, 32, TT), BF16)
        wxr = sbuf(pc4, "wxr", (128, 2, 8, TT), F32)
        wst = sbuf(pc4, "wst", (128, 8, TT), F32)
        sWw = dsem("sWw"); sYl = dsem("sYl"); sXr4 = dsem("sXr4"); sX3 = dsem("sX3")
        tWo4 = role("tWo4", TE); dXo4 = role("dXo4", DV)
        for db in range(2):
            ds = slice(db * 1024, (db + 1) * 1024)
            if db > 0:
                PO.wait_ge(tWo4.h, tWo4.n)
            PO.dma_start(out=wwo[:],
                         in_=woutT.rearrange("(c p) j -> p c j", p=128)[:, :, ds])
            sWw.inc(PO.q[-1])
            TE.wait_ge(sWw.h, sWw.n)
            w40 = tWo4.count(); s30 = sX3.count()
            for tt in range(NTT):
                sl = tt % 2
                ts = slice(tt * TT, (tt + 1) * TT)
                if tt >= 2:
                    SP.wait_ge(tWo4.h, tWo4.val(w40 + (tt - 1) * 8 - 1))
                SP.dma_start(out=wyn[:, sl],
                             in_=yn_scr.rearrange("(c p) t -> p c t", p=128)[:, :, ts])
                sYl.inc(SP.q[-1])
                SP.dma_start(out=wxr[:, sl],
                             in_=x2_scr.rearrange("(c p) t -> p c t", p=128)
                             [:, db * 8:(db + 1) * 8, ts])
                sXr4.inc(SP.q[-1])
                TE.wait_ge(sYl.h, sYl.n)
                DV.wait_ge(sXr4.h, sXr4.n)
                if db * NTT + tt >= 1:
                    DV.wait_ge(sX3.h, sX3.val(-1))
                for dc in range(8):
                    bank = banks[dc % 4]
                    TE.wait_ge(dXo4.h, dXo4.val(dXo4.count() - 4))
                    for k in range(32):
                        TE.matmul(bank[:, :], wwo[:, k, dc * 128:(dc + 1) * 128],
                                  wyn[:, sl, k, :], start=(k == 0), stop=(k == 31))
                    tWo4.inc(TE.q[-1])
                    DV.wait_ge(tWo4.h, tWo4.n)
                    DV.tensor_tensor(wst[:, dc, :], bank[:, :], wxr[:, sl, dc, :],
                                     ALU.add)
                    dXo4.inc(DV.q[-1])
                SP.wait_ge(dXo4.h, dXo4.n)
                SP.dma_start(out=x3_scr.rearrange("(c p) t -> p c t", p=128)
                             [:, db * 8:(db + 1) * 8, ts], in_=wst[:])
                sX3.inc(SP.q[-1])
        SP.wait_ge(sX3.h, sX3.n)
        barrier()
        pc4.close()
        if DBG_DUMP[0]:
            sDmp = dsem("sDmp")
            SP.dma_start(out=zx_out, in_=zx_scr[:, :])
            sDmp.inc(SP.q[-1])
            SP.dma_start(out=dtr_out, in_=dtr_scr[:, :])
            sDmp.inc(SP.q[-1])
            SP.dma_start(out=yn_out, in_=yn_scr[:, :])
            sDmp.inc(SP.q[-1])
            SP.wait_ge(sDmp.h, sDmp.n)

    def finish():
        with nc.Block() as block:
            block.sync(SP.emit)
            block.gpsimd(PO.emit)
            block.tensor(TE.emit)
            block.vector(DV.emit)
            block.scalar(SC.emit)
        gctx.close()
        return nc

    def emit_copy_to_out(src):
        pt = ExitStack()
        ct = sbuf(pt, "ctail", (128, 2, NCD, TT), F32)
        sTl = dsem("sTl"); sTs = dsem("sTs")
        for tt in range(NTT):
            sl = tt % 2
            ts = slice(tt * TT, (tt + 1) * TT)
            if tt >= 2:
                SP.wait_ge(sTs.h, sTs.val(tt - 2))
            SP.dma_start(out=ct[:, sl],
                         in_=src.rearrange("(c p) t -> p c t", p=128)[:, :, ts])
            sTl.inc(SP.q[-1])
            SP.wait_ge(sTl.h, sTl.n)
            SP.dma_start(out=out_fm.rearrange("(c p) t -> p c t", p=128)[:, :, ts],
                         in_=ct[:, sl])
            sTs.inc(SP.q[-1])
        SP.wait_ge(sTs.h, sTs.n)
        pt.close()

    if upto <= 2:
        emit_copy_to_out(x1_scr)
        return finish()
    emit_ffn(x1_scr, x2_scr, "f1")
    if upto == 3:
        emit_copy_to_out(x2_scr)
        return finish()
    emit_mamba_phase()
    if upto == 4:
        emit_copy_to_out(x3_scr)
        return finish()
    emit_ffn(x3_scr, out_fm, "f2")
    return finish()


def _stage_inputs(inputs):
    bf = ml_dtypes.bfloat16
    stage = {
        "wqT": np.ascontiguousarray(np.asarray(inputs["L0_attn_a_W_q"]).T),
        "wdT": np.ascontiguousarray(np.asarray(inputs["L0_attn_a_W_down_kv"]).T),
        "wukT": np.ascontiguousarray(np.asarray(inputs["L0_attn_a_W_up_k"]).transpose(0, 2, 1)),
        "wuvT": np.ascontiguousarray(np.asarray(inputs["L0_attn_a_W_up_v"]).transpose(0, 2, 1)),
        "woT": np.ascontiguousarray(np.asarray(inputs["L0_attn_a_W_o"]).T),
        "winT": np.ascontiguousarray(np.asarray(inputs["L0_attn_b_W_in"]).T),
        "woutT": np.ascontiguousarray(np.asarray(inputs["L0_attn_b_W_out"]).T),
        "g_int": np.ascontiguousarray(np.asarray(inputs["L0_ffn_wq_gate"]).T),
        "u_int": np.ascontiguousarray(np.asarray(inputs["L0_ffn_wq_up"]).T),
        "d_int": np.ascontiguousarray(np.asarray(inputs["L0_ffn_wq_down"]).T),
        "sg_c": np.ascontiguousarray(np.asarray(inputs["L0_ffn_s_gate"]).reshape(NHC, 128).T),
        "su_c": np.ascontiguousarray(np.asarray(inputs["L0_ffn_s_up"]).reshape(NHC, 128).T),
        "sd_c": np.ascontiguousarray(np.asarray(inputs["L0_ffn_s_down"]).reshape(NCD, 128).T),
        "convw_c": np.ascontiguousarray(
            np.asarray(inputs["L0_attn_b_conv_w"]).reshape(33, 128, 4).transpose(1, 0, 2)),
        "convb_c": np.ascontiguousarray(
            np.asarray(inputs["L0_attn_b_conv_b"]).reshape(33, 128).T),
        "dtb_col": np.ascontiguousarray(np.asarray(inputs["L0_attn_b_dt_bias"]).reshape(NH, 1)),
        "alog_col": np.ascontiguousarray(np.asarray(inputs["L0_attn_b_A_log"]).reshape(NH, 1)),
        "dD_big": np.ascontiguousarray(
            np.asarray(inputs["L0_attn_b_D"]).reshape(NCH, 2).T
            .reshape(2, 1, NCH).repeat(64, 1).reshape(128, NCH)),
        "normw_c": np.ascontiguousarray(
            np.asarray(inputs["L0_attn_b_norm_w"]).reshape(32, 128).T),
    }
    stage["c_ident"] = np.eye(128, dtype=bf)
    stage["c_ones_col"] = np.ones((128, 1), dtype=bf)
    stage["c_ones_row_f"] = np.ones((1, 128), dtype=np.float32)
    stage["c_ones64_128"] = np.ones((64, 128), dtype=bf)
    stage["c_ones64_64"] = np.ones((64, 64), dtype=bf)
    stage["c_ut"] = np.triu(np.ones((128, 128), np.float32)).astype(bf)
    stage["c_ltbig"] = np.tril(np.full((128, 128), -1e30, np.float32), -1).T.astype(bf)
    stage["c_i4"] = np.tile(np.eye(128, dtype=np.float32), (1, 4)).astype(bf)
    stage["c_e128"] = np.repeat(np.eye(64, dtype=np.float32), 128, axis=1).astype(bf)
    stage["c_e64"] = np.repeat(np.eye(64, dtype=np.float32), 64, axis=1).astype(bf)
    par = np.zeros((64, 128), np.float32)
    for h in range(64):
        par[h, (h % 2) * 64:(h % 2) * 64 + 64] = 1.0
    stage["c_parity"] = par.astype(bf)
    pm = np.zeros((64, 4096), np.float32)
    for h in range(64):
        pm[h, (h // 2) * 128:(h // 2) * 128 + 128] = 1.0
    stage["c_pairmask"] = pm.astype(bf)
    fc = np.zeros((128, 4), np.float32)
    fc[:, 0] = LN_EPS
    fc[:, 1] = 1.0
    fc[:, 2] = -SM_SHIFT
    stage["c_fconst"] = fc
    return stage


def kernel(**inputs):
    x = np.asarray(inputs["L0_attn_a_x"])
    stage = _stage_inputs(inputs)
    nc = build_program()
    in_maps = []
    for b in range(8):
        m = dict(stage)
        m["x_fm"] = np.ascontiguousarray(x[b].T)
        in_maps.append(m)
    res = run_bass_kernel_spmd(nc, in_maps, core_ids=list(range(8)))
    out = np.stack([r["out_fm"].T for r in res.results], axis=0)
    return out.astype(np.float32)
